# version 01
# baseline (speedup 1.0000x reference)
import contextlib
import math

import numpy as np
import ml_dtypes

import concourse.bass as bass
import concourse.mybir as mybir
from concourse.bass_utils import run_bass_kernel_spmd

F32 = mybir.dt.float32
I32 = mybir.dt.int32
BF16 = mybir.dt.bfloat16
AF = mybir.ActivationFunctionType
OP = mybir.AluOpType
AX = mybir.AxisListType
BF = ml_dtypes.bfloat16

B, T, D = 4, 2048, 2048
DFF = 5504
NH, NKV, HD = 32, 8, 64
NQ = 1024
HALO = 128
NTOK = NQ + HALO
NB = NTOK // 128
MAGIC = 12582912.0
LNK = 30.0
EPS = 1e-5
SM = 1.0 / math.sqrt(HD)
GATE_MC = DFF // 128
NCORES = 8
DSPLIT = [(0, 2688), (2688, DFF)]
MGRP = [list(range(i, min(i + 8, GATE_MC))) for i in range(0, GATE_MC, 8)]

STAGE = "full"
DEBUG = ()
ENGS = ("sync", "tensor", "vector", "scalar", "gpsimd")


def _rope_tables():
    ROPE_THETA = 150000.0
    SCALING = 32.0
    NTK_ALPHA = 1.0
    NTK_BETA = 32.0
    ICL = 1024
    d_half = HD // 2
    freq = ROPE_THETA ** (np.arange(0, HD, 2, dtype=np.float32) / HD)
    concentration = 0.1 * math.log(SCALING) + 1.0
    low = d_half * math.log(ICL / (NTK_BETA * 2 * math.pi)) / math.log(ROPE_THETA)
    high = d_half * math.log(ICL / (NTK_ALPHA * 2 * math.pi)) / math.log(ROPE_THETA)
    interpolation = 1.0 / (SCALING * freq)
    extrapolation = 1.0 / freq
    ramp = np.clip((np.arange(d_half, dtype=np.float32) - low) / (high - low), 0.0, 1.0)
    mask = 1.0 - ramp
    inv_freq = interpolation * (1.0 - mask) + extrapolation * mask
    freqs = np.outer(np.arange(T, dtype=np.float32), inv_freq)
    return (np.cos(freqs) * concentration).astype(np.float32), \
           (np.sin(freqs) * concentration).astype(np.float32)


def _qhead_perm():
    perm = []
    for c in range(16):
        a = 8 * (c // 4) + (c % 4)
        perm += [a, a + 4]
    return perm


PERM_H = _qhead_perm()


def _kb_geom(kb):
    lo = max(0, 128 * (kb - 1))
    hi = min(NQ, 128 * (kb + 1))
    return lo, hi - lo


class EngineProxy:

    def __init__(self, h):
        self._h = h
        self.last = None

    def __getattr__(self, k):
        f = getattr(self._h, k)
        if callable(f):
            def wrap(*a, **kw):
                r = f(*a, **kw)
                if hasattr(r, "then_inc"):
                    self.last = r
                return r
            return wrap
        return f


class W:

    def __init__(self, eng, sems):
        self.eng = eng
        self.h = None
        self.c = {k: 0 for k in sems}
        self.sems = sems

    def inc(self, name, k=1):
        self.c[name] += k
        return self.c[name]

    def wait(self, name, v):
        if self.h is not None:
            self.h.wait_ge(self.sems[name], v)

    def on(self, name):
        return self.eng == name and self.h is not None

    def dr(self):
        if self.h is not None:
            self.h.drain()
            self.h.last = None

    def done(self, name, k=1):
        if self.h is None:
            return
        if self.eng in ("vector", "scalar", "gpsimd"):
            t = self.h.drain()
        else:
            t = self.h.last
            if t is None:
                t = self.h.nop()
        t.then_inc(self.sems[name], k)
        self.h.last = None


def build_nc(stage=STAGE, debug=DEBUG, substage=99):
    ORDER = ["ln", "qkv", "attn", "acc", "full"]
    SI = ORDER.index(stage)

    def on_st(st):
        return ORDER.index(st) <= SI

    nc = bass.Bass("TRN2", target_bir_lowering=False)
    nc.detect_race_conditions = False

    def din(name, shape, dtype=F32):
        return nc.dram_tensor(name, list(shape), dtype, kind="ExternalInput").ap()

    x_fmh = din("x_fm", [16, 128, NTOK])
    x_tmh = din("x_tm", [NB, 128, D])
    qkvwh = din("qkvw", [24, D, 128])
    qkvbh = din("qkvb", [1, 24 * 128], BF16)
    outwh = din("outw", [16, D, 128])
    outbh = din("outb", [1, 16 * 128], BF16)
    wgh = din("wg", [GATE_MC, 128, D])
    wuh = din("wu", [GATE_MC, 128, D])
    wdh = din("wd", [16, 128, DFF])
    cqh = din("cq", [128, 2 * NQ])
    ckh = din("ck", [128, 2 * NTOK])
    pswaph = din("pswap", [128, 128], BF16)
    pswapfh = din("pswapf", [128, 128])
    m0h = din("m0", [128, 128], BF16)
    m1h = din("m1", [128, 256], BF16)
    identbh = din("identb", [128, 128], BF16)
    identfh = din("identf", [128, 128], F32)
    onesfh = din("onesf", [128, 128], F32)
    onesrh = din("onesr", [1, 512], BF16)
    onescolh = din("onescol", [128, 1])
    sinksh = din("sinks", [1, 32], F32)
    wvth = din("wvt", [16, 128, 512])

    wq_gh = nc.dram_tensor("wq_g", [DFF, D], BF16).ap()
    wq_uh = nc.dram_tensor("wq_u", [DFF, D], BF16).ap()
    wq_dh = nc.dram_tensor("wq_d", [D, DFF], BF16).ap()
    acc_hh = nc.dram_tensor("acc_h", [16, 128, NQ], F32).ap()
    hsp_hh = nc.dram_tensor("hsp_h", [GATE_MC, 128, NQ], F32).ap()
    qh_hh = nc.dram_tensor("qh_h", [4, GATE_MC, 128, 256], BF16).ap()
    qa_hh = nc.dram_tensor("qa_h", [16, 128, NQ], BF16).ap()

    yh = nc.dram_tensor("y", [16, 128, NQ], F32, kind="ExternalOutput").ap()

    dbg = {}
    for nm, shape, dt_, gate in (
        ("d_abc", [128, 2 * NTOK], F32, "ln"),
        ("d_sc", [128, 24], F32, "ln"),
        ("d_qp", [128, 16 * NQ], BF16, "qkv"),
        ("d_kp", [128, 4 * NTOK], BF16, "qkv"),
        ("d_vsb", [128, NB * NKV * 65], BF16, "qkv"),
        ("d_xnbf", [128, 16 * NTOK], BF16, "qkv"),
        ("d_ap", [128, 16 * NQ], BF16, "attn"),
        ("d_cval", [1, 8], F32, "attn"),
        ("d_qi", [128, 16 * NQ], BF16, "qint"),
        ("d_acc", [16, 128, NQ], F32, "acc"),
        ("d_wqg", [DFF, D], BF16, "wq"),
    ):
        if gate in debug:
            dbg[nm] = nc.dram_tensor(nm, shape, dt_, kind="ExternalOutput").ap()

    es = contextlib.ExitStack()
    with es:
        def sb(name, shape, dtype=F32):
            return es.enter_context(nc.sbuf_tensor(name, list(shape), dtype))

        def psu(name, shape, dtype=F32):
            return es.enter_context(nc.psum_tensor(name, list(shape), dtype))

        SEMN = ["c", "x", "w", "wf", "t", "v", "s", "g", "fw", "fwf", "wq",
                "wqw", "wqf", "sl", "sp", "hr", "st"]
        sems = {n: es.enter_context(nc.semaphore("sm_" + n)) for n in SEMN}
        Ws = {e: W(e, sems) for e in ENGS}

        def phase(fn):
            with nc.Block() as blk:
                for e in ENGS:
                    w = Ws[e]

                    def section(h, w=w):
                        w.h = EngineProxy(h)
                        fn(w)
                        w.h = None
                    getattr(blk, e)(section)

        cb = sb("cb", [128, 128 * 3 + 256], BF16)
        pswap, identb = cb[:, 0:128], cb[:, 128:256]
        m0, m1 = cb[:, 256:384], cb[:, 384:640]
        onescol = sb("onescol_s", [128, 1])[:, :]
        onesr = sb("onesr_s", [1, 512], BF16)[:, :]
        identf = sb("identf_s", [128, 128])[:, :]
        onesf = sb("onesf_s", [128, 128])[:, :]
        pswf = sb("pswf_s", [128, 128])[:, :]
        biasb = sb("biasb", [1, 40 * 128], BF16)
        qbias = biasb[:, 0:24 * 128]
        obias = biasb[:, 24 * 128:40 * 128]
        sinksr = sb("sinksr", [1, 32])[:, :]
        es_b = sb("es_b", [128, 32])
        cvec = sb("cvec", [128, 2])
        negc = cvec[:, 0:1]
        negcK = cvec[:, 1:2]
        mvt = sb("mvt", [128, 2 * NB])
        abcol = sb("abcol", [128, 2 * NB])
        scq = sb("scq", [128, 24])
        abrow = sb("abrow", [1, 2 * NTOK])[:, :]
        scrow = sb("scrow", [1, 2 * NQ])[:, :]
        qnrow = sb("qnrow", [1, 64])[:, :]
        tcol = sb("tcol", [128, 8])

        bank = [psu(f"bk{i}", [128, 512]) for i in range(7)]
        psV = psu("psV", [128, 512], F32)
        psR = bank[6]

        MS = {}

        def mark(w, key, semname, k=1):
            v = w.inc(semname, k)
            MS[key] = (semname, v)
            return v

        def waitm(w, key):
            sn, v = MS[key]
            w.wait(sn, v)

        DMASEMS = ("c", "x", "w", "fw", "wqw", "sl", "sp", "hr", "st")

        def dma_fence(w, engines=("sync", "gpsimd")):
            if w.eng in engines and w.h is not None:
                for sn in DMASEMS:
                    if w.c[sn]:
                        w.h.wait_ge(sems[sn], w.c[sn])

        def ph_const(w):
            if w.on("sync"):
                h = w.h
                h.dma_start(pswap, pswaph).then_inc(sems["c"], 16)
                h.dma_start(identb, identbh).then_inc(sems["c"], 16)
                h.dma_start(m0, m0h).then_inc(sems["c"], 16)
                h.dma_start(m1, m1h).then_inc(sems["c"], 16)
                h.dma_start(onescol, onescolh).then_inc(sems["c"], 16)
                h.dma_start(onesr, onesrh).then_inc(sems["c"], 16)
                h.dma_start(identf, identfh).then_inc(sems["c"], 16)
                h.dma_start(onesf, onesfh).then_inc(sems["c"], 16)
                h.dma_start(pswf, pswapfh).then_inc(sems["c"], 16)
                h.dma_start(sinksr, sinksh).then_inc(sems["c"], 16)
                h.dma_start(qbias, qkvbh).then_inc(sems["c"], 16)
                h.dma_start(obias, outbh).then_inc(sems["c"], 16)
            mark(w, "const", "c", 16 * 12)
        phase(ph_const)

        with contextlib.ExitStack() as sA:
            xtm = sA.enter_context(nc.sbuf_tensor("xtm", [128, NB * D], F32))
            stats = sA.enter_context(nc.sbuf_tensor("stats", [128, NB * 24], F32))

            def ph_A(w):
                if substage < 1:
                    MS["rows"] = MS["const"]
                    return
                if w.on("sync"):
                    for i in range(NB):
                        w.h.dma_start(xtm[:, i * D:(i + 1) * D],
                                      x_tmh[i]).then_inc(sems["x"], 16)
                mark(w, "xtm", "x", 16 * NB)

                if w.on("vector"):
                    waitm(w, "xtm")
                    w.h.memset(tcol[:, 3:4], EPS)
                    w.h.memset(tcol[:, 4:5], LNK)
                    for i in range(NB):
                        for sg_ in range(4):
                            w.h.bn_stats(
                                stats[:, i * 24 + sg_ * 6: i * 24 + (sg_ + 1) * 6],
                                xtm[:, i * D + sg_ * 512: i * D + (sg_ + 1) * 512])
                    w.dr()
                    for i in range(NB):
                        w.h.bn_aggr(
                            mvt[:, 2 * i:2 * i + 2],
                            stats[:, i * 24:(i + 1) * 24]
                            .rearrange("p (k t) -> p k t", t=3))
                    w.done("v", 1)
                mark(w, "stats", "v")
                if substage == 10:
                    MS["rows"] = MS["stats"]
                    return

                if w.on("scalar"):
                    waitm(w, "stats")
                    for i in range(NB):
                        w.h.activation(tcol[:, 0:1], mvt[:, 2 * i + 1:2 * i + 2],
                                       AF.Ln, bias=tcol[:, 3:4])
                        w.dr()
                        w.h.activation(abcol[:, i:i + 1], tcol[:, 0:1],
                                       AF.Exp, scale=-0.5)
                        w.dr()
                    w.done("s", 1)
                mark(w, "rstd", "s")
                if substage == 11:
                    MS["rows"] = MS["rstd"]
                    return

                if w.on("vector"):
                    waitm(w, "rstd")
                    w.dr()
                    for i in range(NB):
                        w.h.scalar_tensor_tensor(
                            abcol[:, NB + i:NB + i + 1], mvt[:, 2 * i:2 * i + 1],
                            -1.0, abcol[:, i:i + 1], OP.mult, OP.mult)
                    for i in range(1, NB):
                        j = i - 1
                        w.h.tensor_scalar(
                            xtm[:, i * D:(i + 1) * D], xtm[:, i * D:(i + 1) * D],
                            mvt[:, 2 * i:2 * i + 1], None, OP.subtract)
                        w.h.tensor_reduce(
                            tcol[:, 0:1], xtm[:, i * D:(i + 1) * D], AX.X, OP.max,
                            apply_absolute_value=True)
                        w.dr()
                        w.h.scalar_tensor_tensor(
                            tcol[:, 1:2], tcol[:, 0:1], 1.0, abcol[:, i:i + 1],
                            OP.bypass, OP.mult)
                        w.dr()
                        w.h.tensor_scalar(scq[:, j:j + 1], tcol[:, 1:2], EPS,
                                          None, OP.max)
                        w.dr()
                        w.h.reciprocal(tcol[:, 2:3], scq[:, j:j + 1])
                        w.dr()
                        w.h.tensor_scalar(scq[:, 8 + j:9 + j], tcol[:, 2:3],
                                          127.0, None, OP.mult)
                        w.dr()
                        w.h.reciprocal(scq[:, 16 + j:17 + j], scq[:, 8 + j:9 + j])
                        w.dr()
                    w.done("v", 1)
                mark(w, "scq", "v")
                if substage == 12:
                    MS["rows"] = MS["scq"]
                    return

                cols = [(abcol[:, i:i + 1], abrow[:, i * 128:(i + 1) * 128])
                        for i in range(NB)]
                cols += [(abcol[:, NB + i:NB + i + 1],
                          abrow[:, NTOK + i * 128: NTOK + (i + 1) * 128])
                         for i in range(NB)]
                cols += [(scq[:, 8 + j:9 + j], scrow[:, j * 128:(j + 1) * 128])
                         for j in range(8)]
                cols += [(scq[:, 16 + j:17 + j],
                          scrow[:, NQ + j * 128: NQ + (j + 1) * 128])
                         for j in range(8)]
                if substage in (14, 141, 16, 17, 18):
                    cols = cols[:3]
                elif substage == 15:
                    cols = cols[:18]
                for g0 in range(0, len(cols), 3):
                    grp = cols[g0:g0 + 3]
                    if w.on("tensor"):
                        if g0 == 0:
                            waitm(w, "scq")
                            waitm(w, "const")
                        else:
                            waitm(w, ("rowcp", g0 - 3))
                        if substage != 17:
                            for ci, (src, _d) in enumerate(grp):
                                w.h.transpose(psR[0:1, ci * 128:(ci + 1) * 128],
                                              src, identf)
                        w.done("t", 1)
                    mark(w, ("rowtr", g0), "t")
                    ceng = "vector" if substage == 18 else "scalar"
                    if w.on(ceng):
                        waitm(w, ("rowtr", g0))
                        if substage != 16:
                            for ci, (_s, dst) in enumerate(grp):
                                if ceng == "vector":
                                    w.h.tensor_copy(dst, psR[0:1, ci * 128:(ci + 1) * 128])
                                else:
                                    w.h.copy(dst, psR[0:1, ci * 128:(ci + 1) * 128])
                        w.done("s", 1)
                    mark(w, ("rowcp", g0), "s")
                MS["rows"] = MS[("rowcp", g0)]
            phase(ph_A)

        sB = contextlib.ExitStack()

        def sbB(name, shape, dtype=F32):
            return sB.enter_context(nc.sbuf_tensor(name, list(shape), dtype))

        qp = sbB("qp", [128, 16 * NQ], BF16)
        kp = sbB("kp", [128, 4 * NTOK], BF16)
        vsb = sbB("vsb", [128, NB * NKV * 65], BF16)
        abc = sbB("abc", [128, 2 * NTOK], F32)
        wsc = sbB("wsc", [128, 4 * 22], F32)

        def ph_B0(w):
            if substage < 20 and substage != 99:
                return
            if w.on("gpsimd"):
                waitm(w, "const")
                w.h.memset(vsb[:, :], 1.0)
                w.done("g", 1)
            mark(w, "vones", "g")

            for t_ in range(2 * NB):
                half, i = divmod(t_, NB)
                if w.on("tensor"):
                    if t_ == 0:
                        waitm(w, "rows")
                    else:
                        waitm(w, ("abc_cp", t_ - 1))
                    w.h.matmul(
                        bank[5][:, 0:128], onesf[0:1, :],
                        abrow[:, half * NTOK + i * 128: half * NTOK + (i + 1) * 128],
                        start=True, stop=True).then_inc(sems["t"], 1)
                mark(w, ("abc_mm", t_), "t")
                if w.on("vector"):
                    waitm(w, ("abc_mm", t_))
                    w.h.tensor_copy(
                        abc[:, half * NTOK + i * 128: half * NTOK + (i + 1) * 128],
                        bank[5][:, 0:128])
                    w.done("v", 1)
                mark(w, ("abc_cp", t_), "v")
            MS["abc"] = MS[("abc_cp", 2 * NB - 1)]

            if "d_abc" in dbg:
                if w.on("sync"):
                    waitm(w, "abc")
                    w.h.dma_start(dbg["d_abc"], abc[:, :]).then_inc(sems["st"], 16)
                    w.h.dma_start(dbg["d_sc"], scq[:, :]).then_inc(sems["st"], 16)
                w.inc("st", 32)
        phase(ph_B0)

        if on_st("qkv"):
            with contextlib.ExitStack() as s1:
                def sb1(name, shape, dtype=F32):
                    return s1.enter_context(nc.sbuf_tensor(name, list(shape), dtype))

                xf = sb1("xf", [128, 2 * NTOK], F32)
                xnf = sb1("xnf", [128, NTOK], F32)
                xnbf = sb1("xnbf", [128, 16 * NTOK], BF16)
                cq = sb1("cq_s", [128, 2 * NQ], F32)
                ck = sb1("ck_s", [128, 2 * NTOK], F32)
                evb = sb1("evb", [128, 2 * NTOK], F32)
                qsw = sb1("qsw", [128, NTOK], F32)
                tta = sb1("tta", [128, NTOK], F32)
                ttb = sb1("ttb", [128, NTOK], F32)
                wbuf = sb1("wbuf", [128, 2 * 2048], BF16)
                wvb = sb1("wvb", [128, 16 * 512], BF16)[:, :]

                def ph_B1(w):
                    dma_fence(w)
                    if w.on("sync"):
                        w.h.dma_start(cq[:, :], cqh).then_inc(sems["c"], 16)
                        w.h.dma_start(ck[:, :], ckh).then_inc(sems["c"], 16)
                    mark(w, "tables", "c", 32)

                    for c in range(16):
                        if w.on("sync"):
                            if c >= 2:
                                waitm(w, ("xn", c - 2))
                            w.h.dma_start(xf[:, (c % 2) * NTOK:(c % 2 + 1) * NTOK],
                                          x_fmh[c]).then_inc(sems["x"], 16)
                        mark(w, ("xld", c), "x", 16)
                        if w.on("vector"):
                            waitm(w, ("xld", c))
                            if c == 0:
                                waitm(w, "abc")
                            w.h.tensor_mul(
                                xnf[:, :], xf[:, (c % 2) * NTOK:(c % 2 + 1) * NTOK],
                                abc[:, 0:NTOK])
                            w.h.tensor_add(
                                xnbf[:, c * NTOK:(c + 1) * NTOK], xnf[:, :],
                                abc[:, NTOK:2 * NTOK])
                            w.done("v", 1)
                        mark(w, ("xn", c), "v")
                    MS["xn_all"] = MS[("xn", 15)]
                    if substage == 21:
                        return

                    WLIST = [0, 1, 2, 3] + list(range(8, 24))
                    if substage == 22:
                        WLIST = [0, 1, 2, 3]
                    elif substage == 27:
                        WLIST = [0, 1, 2, 3, 8]
                    for wi, mb in enumerate(WLIST):
                        if w.on("gpsimd"):
                            if wi >= 2:
                                waitm(w, ("wfree", wi - 2))
                            w.h.dma_start(
                                wbuf[:, (wi % 2) * 2048:(wi % 2 + 1) * 2048]
                                .rearrange("p (k n) -> p k n", k=16),
                                qkvwh[mb].rearrange("(k p) n -> p k n", p=128))\
                                .then_inc(sems["w"], 16)
                        mark(w, ("wld", wi), "w", 16)
                        if wi == 3 and substage != 22:
                            if w.on("gpsimd"):
                                w.h.dma_start(
                                    wvb.rearrange("p (k n) -> p k n", k=16),
                                    wvth.rearrange("k p n -> p k n"))\
                                    .then_inc(sems["w"], 16)
                            mark(w, "wvld", "w", 16)

                    SCHED = [("kq", i, m) for i, m in enumerate([0, 1, 2, 3])]
                    vb = 0
                    for j, mb in enumerate(range(8, 24)):
                        if j % 2 == 0 and vb < NB:
                            SCHED.append(("v", vb, None))
                            vb += 1
                        SCHED.append(("kq", 4 + j, mb))
                    while vb < NB:
                        SCHED.append(("v", vb, None))
                        vb += 1
                    if substage == 22:
                        SCHED = SCHED[:4]
                    elif substage == 23:
                        SCHED = SCHED[:4] + [e for e in SCHED if e[0] == "v"]
                    elif substage == 27:
                        SCHED = SCHED[:6]

                    nseg = 0
                    nrp = 0
                    NPS = [0]
                    for ent in SCHED:
                        if ent[0] == "v":
                            tb = ent[1]
                            if w.on("tensor"):
                                waitm(w, "wvld")
                                if tb == 0:
                                    waitm(w, "xn_all")
                                else:
                                    waitm(w, ("vev", tb - 1))
                                for kc in range(16):
                                    w.h.matmul(
                                        psV[:, 0:512],
                                        xnbf[:, kc * NTOK + tb * 128:
                                             kc * NTOK + (tb + 1) * 128],
                                        wvb[:, kc * 512:(kc + 1) * 512],
                                        start=(kc == 0), stop=False)
                                w.h.matmul(
                                    psV[:, 0:512], onesr[:, 0:128],
                                    biasb[:, 512:1024], start=False, stop=True)\
                                    .then_inc(sems["t"], 1)
                            mark(w, ("vmm", tb), "t")
                            if w.on("scalar"):
                                waitm(w, ("vmm", tb))
                                if tb == 0:
                                    waitm(w, "vones")
                                for hh in range(NKV):
                                    w.h.copy(
                                        vsb[:, (tb * NKV + hh) * 65:
                                            (tb * NKV + hh) * 65 + 64],
                                        psV[:, hh * 64:(hh + 1) * 64])
                                w.done("s", 1)
                            mark(w, ("vev", tb), "s")
                            continue

                        _, wi, mb = ent
                        iskv = mb < 8
                        segs = [(0, 512), (512, 512), (1024, 128)] if iskv \
                            else [(128, 512), (640, 512)]
                        blo = 0 if iskv else HALO
                        for (lo, ln_) in segs:
                            if w.on("tensor"):
                                if nseg == 0:
                                    waitm(w, "xn_all")
                                waitm(w, ("wld", wi))
                                if nseg >= 2:
                                    waitm(w, ("ev", nseg - 2))
                                bk = bank[nseg % 2]
                                for kc in range(16):
                                    w.h.matmul(
                                        bk[:, 0:ln_],
                                        wbuf[:, (wi % 2) * 2048 + kc * 128:
                                             (wi % 2) * 2048 + (kc + 1) * 128],
                                        xnbf[:, kc * NTOK + lo: kc * NTOK + lo + ln_],
                                        start=(kc == 0), stop=False)
                                w.h.matmul(
                                    bk[:, 0:ln_], qbias[:, mb * 128:(mb + 1) * 128],
                                    onesr[:, 0:ln_], start=False, stop=True)\
                                    .then_inc(sems["t"], 1)
                            mark(w, ("qmm", nseg), "t")
                            if w.on("scalar"):
                                waitm(w, ("qmm", nseg))
                                if lo == segs[0][0] and wi >= 2:
                                    waitm(w, ("post", wi - 2))
                                w.h.copy(
                                    evb[:, (wi % 2) * NTOK + lo - blo:
                                        (wi % 2) * NTOK + lo - blo + ln_],
                                    bank[nseg % 2][:, 0:ln_])
                                w.done("s", 1)
                            mark(w, ("ev", nseg), "s")
                            nseg += 1
                        MS[("wfree", wi)] = MS[("qmm", nseg - 1)]
                        MS[("evblk", mb)] = MS[("ev", nseg - 1)]

                        isq = mb >= 8
                        ntk = NQ if isq else NTOK
                        cst, sst = (cq[:, 0:NQ], cq[:, NQ:2 * NQ]) if isq \
                            else (ck[:, 0:NTOK], ck[:, NTOK:2 * NTOK])
                        dst = qp[:, (mb - 8) * NQ:(mb - 7) * NQ] if isq \
                            else kp[:, mb * NTOK:(mb + 1) * NTOK]
                        for j, off in enumerate(range(0, ntk, 512)):
                            wd_ = min(512, ntk - off)
                            nps = NPS[0]
                            if w.on("tensor"):
                                if j == 0:
                                    waitm(w, ("evblk", mb))
                                if nps >= 2:
                                    waitm(w, ("qswe_s", nps - 2))
                                w.h.matmul(bank[2 + nps % 2][:, 0:wd_], pswf,
                                           evb[:, (wi % 2) * NTOK + off:
                                               (wi % 2) * NTOK + off + wd_],
                                           start=True, stop=True)\
                                    .then_inc(sems["t"], 1)
                            mark(w, ("psw_s", nps), "t")
                            if w.on("scalar"):
                                waitm(w, ("psw_s", nps))
                                if j == 0 and nrp >= 1:
                                    waitm(w, ("ropeN", nrp - 1))
                                w.h.copy(qsw[:, off:off + wd_],
                                         bank[2 + nps % 2][:, 0:wd_])
                                w.done("s", 1)
                            mark(w, ("qswe_s", nps), "s")
                            NPS[0] += 1
                        MS[("qswe", nrp)] = MS[("qswe_s", NPS[0] - 1)]
                        if w.on("vector"):
                            waitm(w, ("qswe", nrp))
                            w.h.tensor_mul(tta[:, 0:ntk],
                                           evb[:, (wi % 2) * NTOK:
                                               (wi % 2) * NTOK + ntk], cst)
                            w.h.tensor_mul(ttb[:, 0:ntk], qsw[:, 0:ntk], sst)
                            w.h.tensor_add(dst, tta[:, 0:ntk], ttb[:, 0:ntk])
                            w.done("v", 1)
                        mark(w, ("rope", mb), "v")
                        MS[("post", wi)] = MS[("rope", mb)]
                        MS[("ropeN", nrp)] = MS[("rope", mb)]
                        nrp += 1

                    if substage == 27:
                        if w.on("sync"):
                            waitm(w, ("rope", 8))
                            w.h.dma_start(dbg["d_qp"], qp[:, :]).then_inc(sems["st"], 16)
                            w.h.dma_start(dbg["d_kp"], kp[:, :]).then_inc(sems["st"], 16)
                            w.h.dma_start(
                                dbg["d_xnbf"][:, 0:NTOK],
                                evb[:, 0:NTOK])\
                                .then_inc(sems["st"], 16)
                            w.h.dma_start(
                                dbg["d_xnbf"][:, NTOK:2 * NTOK],
                                qsw[:, 0:NTOK]).then_inc(sems["st"], 16)
                            w.h.dma_start(
                                dbg["d_xnbf"][:, 2 * NTOK:3 * NTOK],
                                tta[:, 0:NTOK]).then_inc(sems["st"], 16)
                            w.h.dma_start(
                                dbg["d_xnbf"][:, 3 * NTOK:4 * NTOK],
                                ttb[:, 0:NTOK]).then_inc(sems["st"], 16)
                        w.inc("st", 16 * 5)
                        return
                    if substage in (22, 23, 24):
                        return
                    cnt_qn = 0
                    lastmm_of = {}
                    for ci in range(20):
                        isq = ci < 16
                        mb = (8 + ci) if isq else (ci - 16)
                        src_ = qp[:, ci * NQ:(ci + 1) * NQ] if isq \
                            else kp[:, (ci - 16) * NTOK:(ci - 15) * NTOK]
                        ntk = NQ if isq else NTOK
                        sqt = tta if ci % 2 == 0 else ttb
                        if w.on("vector"):
                            waitm(w, ("rope", mb))
                            if ci >= 2:
                                waitm(w, ("qn_mmS", lastmm_of[ci - 2]))
                            w.h.tensor_mul(sqt[:, 0:ntk], src_, src_)
                            w.done("v", 1)
                        mark(w, ("qn_sq", ci), "v")
                        for off in range(0, ntk, 512):
                            wd_ = min(512, ntk - off)
                            if w.on("tensor"):
                                waitm(w, ("qn_sq", ci))
                                if cnt_qn >= 2:
                                    waitm(w, ("qn_rm", cnt_qn - 2))
                                w.h.matmul(bank[4 + cnt_qn % 2][0:1, 0:wd_],
                                           onescol, sqt[:, off:off + wd_],
                                           start=True, stop=True)
                                w.done("t", 1)
                            mark(w, ("qn_mmS", cnt_qn), "t")
                            if w.on("vector"):
                                waitm(w, ("qn_mmS", cnt_qn))
                                w.h.tensor_reduce(qnrow[:, cnt_qn:cnt_qn + 1],
                                                  bank[4 + cnt_qn % 2][0:1, 0:wd_],
                                                  AX.X, OP.max)
                                w.done("v", 1)
                            mark(w, ("qn_rm", cnt_qn), "v")
                            lastmm_of[ci] = cnt_qn
                            cnt_qn += 1
                    NQN = cnt_qn
                    if w.on("vector"):
                        waitm(w, ("qn_rm", NQN - 1))
                        w.dr()
                        w.h.tensor_reduce(qnrow[:, 44:45], qnrow[:, 0:32], AX.X, OP.max)
                        w.h.tensor_reduce(qnrow[:, 45:46], qnrow[:, 32:44], AX.X, OP.max)
                        w.done("v", 1)
                    mark(w, "qn_red", "v")
                    if w.on("scalar"):
                        waitm(w, "qn_red")
                        w.h.activation(qnrow[:, 46:47], qnrow[:, 44:45], AF.Ln)
                        w.dr()
                        w.h.activation(qnrow[:, 47:48], qnrow[:, 45:46], AF.Ln)
                        w.done("s", 1)
                    mark(w, "qn_ln", "s")
                    if w.on("vector"):
                        waitm(w, "qn_ln")
                        w.h.tensor_add(qnrow[:, 48:49], qnrow[:, 46:47], qnrow[:, 47:48])
                        w.done("v", 1)
                    mark(w, "qn_add", "v")
                    if w.on("scalar"):
                        waitm(w, "qn_add")
                        w.h.activation(qnrow[:, 49:50], qnrow[:, 48:49], AF.Exp, scale=0.5)
                        w.done("s", 1)
                    mark(w, "qn_exp", "s")
                    if w.on("vector"):
                        waitm(w, "qn_exp")
                        w.h.tensor_scalar(qnrow[:, 50:51], qnrow[:, 49:50],
                                          1.01, -79.0, OP.mult, OP.add)
                        w.dr()
                        w.h.tensor_scalar(qnrow[:, 52:53], qnrow[:, 50:51],
                                          0.0, None, OP.max)
                        w.dr()
                        w.h.tensor_scalar(qnrow[:, 51:52], qnrow[:, 52:53],
                                          -1.0, None, OP.mult)
                        w.dr()
                        w.h.tensor_scalar(qnrow[:, 53:54], qnrow[:, 51:52],
                                          1.0, LNK, OP.mult, OP.add)
                        w.done("v", 1)
                    mark(w, "cneg", "v")
                    if substage == 25:
                        return
                    if w.on("tensor"):
                        waitm(w, "cneg")
                        w.h.matmul(bank[4][:, 0:1], onesf[0:1, :],
                                   qnrow[:, 51:52], start=True, stop=True)
                        w.h.matmul(bank[4][:, 4:5], onesf[0:1, :],
                                   qnrow[:, 53:54], start=True, stop=True)
                        w.h.matmul(bank[5][:, 0:32], onesf[0:1, :],
                                   sinksr, start=True, stop=True)\
                            .then_inc(sems["t"], 1)
                    mark(w, "cbcmm", "t")
                    if w.on("scalar"):
                        waitm(w, "cbcmm")
                        w.h.copy(negc, bank[4][:, 0:1])
                        w.h.copy(negcK, bank[4][:, 4:5])
                        w.h.copy(es_b[:, 0:32], bank[5][:, 0:32])
                        w.done("s", 1)
                    mark(w, "cbc", "s")
                    if w.on("scalar"):
                        waitm(w, "cbc")
                        w.dr()
                        w.h.activation(es_b[:, 0:32], es_b[:, 0:32], AF.Exp,
                                       bias=negcK)
                        w.done("s", 1)
                    mark(w, "esb", "s")

                    if "d_qp" in dbg:
                        if w.on("sync"):
                            for mb in range(8, 24):
                                waitm(w, ("rope", mb))
                            waitm(w, ("vev", NB - 1))
                            waitm(w, "esb")
                            w.h.dma_start(dbg["d_qp"], qp[:, :]).then_inc(sems["st"], 16)
                            w.h.dma_start(dbg["d_kp"], kp[:, :]).then_inc(sems["st"], 16)
                            w.h.dma_start(dbg["d_vsb"], vsb[:, :]).then_inc(sems["st"], 16)
                            w.h.dma_start(dbg["d_xnbf"], xnbf[:, :]).then_inc(sems["st"], 16)
                            if "d_cval" in dbg:
                                w.h.dma_start(dbg["d_cval"],
                                              qnrow[:, 44:52]).then_inc(sems["st"], 16)
                        w.inc("st", 16 * (5 if "d_cval" in dbg else 4))
                phase(ph_B1)

        GSPL = [(0, 1024), (1024, 2048)]
        DSPL4 = [(0, 1408), (1408, 2816), (2816, 4224), (4224, DFF)]
        QUNITS = [("g", i, sp, (sp[1] - sp[0]) // 128)
                  for i in range(GATE_MC) for sp in GSPL] \
            + [("u", i, sp, (sp[1] - sp[0]) // 128)
               for i in range(GATE_MC) for sp in GSPL] \
            + [("d", i, sp, (sp[1] - sp[0]) // 128)
               for i in range(16) for sp in DSPL4]
        NQU = len(QUNITS)
        NGU_G = 2 * GATE_MC

        def qsrc(u):
            kind, i, (lo, hi), _g = u
            if kind == "g":
                return wgh[i][:, lo:hi]
            if kind == "u":
                return wuh[i][:, lo:hi]
            return wdh[i][:, lo:hi]

        def qdst(u):
            kind, i, (lo, hi), _g = u
            if kind == "g":
                return wq_gh[128 * i:128 * (i + 1), lo:hi]
            if kind == "u":
                return wq_uh[128 * i:128 * (i + 1), lo:hi]
            return wq_dh[128 * i:128 * (i + 1), lo:hi]

        def qw_emit(w, upto):
            cur = getattr(w, "qcur", 0)
            while cur < min(upto, NQU):
                n = cur
                u = QUNITS[n]
                _k, _i, (lo, hi), G = u
                spl = hi - lo
                slot = (n % 2) * 1408
                if w.on("sync"):
                    if n >= 2:
                        waitm(w, ("qgp", n - 2))
                    w.h.dma_start(fwbuf[:, slot:slot + spl], qsrc(u))\
                        .then_inc(sems["fw"], 16)
                mark(w, ("qfw", n), "fw", 16)
                if w.on("vector"):
                    waitm(w, ("qfw", n))
                    w.h.tensor_reduce(
                        wsc[:, 0:G],
                        fwbuf[:, slot:slot + spl].rearrange("p (g k) -> p g k", k=128),
                        AX.X, OP.add, apply_absolute_value=True)
                    w.dr()
                    w.h.tensor_scalar(wsc[:, 22:22 + G], wsc[:, 0:G],
                                      1.0 / 128.0, EPS, OP.mult, OP.add)
                    w.dr()
                    w.h.tensor_scalar(wsc[:, 44:44 + G], wsc[:, 22:22 + G],
                                      0.5, None, OP.mult)
                    w.h.tensor_scalar(wsc[:, 66:66 + G], wsc[:, 22:22 + G],
                                      -0.5, None, OP.mult)
                    w.dr()
                    w.done("v", 1)
                mark(w, ("qthr", n), "v")
                if w.on("gpsimd"):
                    waitm(w, ("qthr", n))
                    if n >= 1:
                        waitm(w, ("qcomb", n - 1))
                    for g in range(G):
                        gs = slice(g * 128, (g + 1) * 128)
                        w.h.tensor_scalar(
                            d1s[:, gs], fwbuf[:, slot + g * 128: slot + (g + 1) * 128],
                            wsc[:, 44 + g:45 + g], wsc[:, 22 + g:23 + g],
                            OP.is_gt, OP.mult)
                        w.h.tensor_scalar(
                            d2s[:, gs], fwbuf[:, slot + g * 128: slot + (g + 1) * 128],
                            wsc[:, 66 + g:67 + g], wsc[:, 22 + g:23 + g],
                            OP.is_lt, OP.mult)
                    w.done("g", 1)
                mark(w, ("qgp", n), "g")
                if w.on("vector"):
                    waitm(w, ("qgp", n))
                    if n >= 2:
                        waitm(w, ("qwqw", n - 2))
                    w.h.tensor_sub(wqrm[:, slot:slot + spl], d1s[:, 0:spl],
                                   d2s[:, 0:spl])
                    w.done("v", 1)
                mark(w, ("qcomb", n), "v")
                if w.on("sync"):
                    waitm(w, ("qcomb", n))
                    w.h.dma_start(qdst(u), wqrm[:, slot:slot + spl])\
                        .then_inc(sems["wqw"], 16)
                mark(w, ("qwqw", n), "wqw", 16)
                cur += 1
            w.qcur = cur

        if on_st("attn"):
            with contextlib.ExitStack() as s2:
                def sb2(name, shape, dtype=F32):
                    return s2.enter_context(nc.sbuf_tensor(name, list(shape), dtype))

                etb = sb2("etb", [128, 2 * 256], BF16)
                nrm = sb2("nrm", [128, 512], F32)
                owbuf = sb2("owbuf", [128, 2 * 2048], BF16)
                xf2 = sb2("xf2", [128, 2 * NQ], F32)
                accb = sb2("accb", [128, 2 * NQ], F32)
                scbc = sb2("scbc", [128, 2 * NQ], F32)
                xnf2 = sb2("xnf2", [128, 2 * NQ], F32)
                qi32 = sb2("qi32", [128, NQ], I32)
                ap_ = sb2("ap", [128, 16 * NQ], BF16)
                fwbuf = sb2("fwbuf", [128, 2 * 1408], F32)
                wqrm = sb2("wqrm", [128, 2 * 1408], BF16)
                d1s = sb2("d1s", [128, 1408], BF16)
                d2s = sb2("d2s", [128, 1408], BF16)
                qtr = sb2("qtr", [128, 4 * NQ], BF16)

                def ph_B2(w):
                    dma_fence(w)
                    w.qcur = 0
                    for t_ in range(4):
                        half = t_ // 2
                        off = (t_ % 2) * 512
                        if w.on("tensor"):
                            if t_ == 0:
                                waitm(w, "rows")
                                waitm(w, "abc")
                                waitm(w, ("qn_rm", 43))
                            else:
                                waitm(w, ("scbc_cp", t_ - 1))
                            w.h.matmul(
                                bank[5][:, 0:512], onesf[0:1, :],
                                scrow[:, half * NQ + off: half * NQ + off + 512],
                                start=True, stop=True).then_inc(sems["t"], 1)
                        mark(w, ("scbc_mm", t_), "t")
                        if w.on("vector"):
                            waitm(w, ("scbc_mm", t_))
                            w.h.tensor_copy(
                                scbc[:, half * NQ + off: half * NQ + off + 512],
                                bank[5][:, 0:512])
                            w.done("v", 1)
                        mark(w, ("scbc_cp", t_), "v")

                    SLIM = 28 if substage == 28 else 32
                    for s_ in range(SLIM):
                        h = PERM_H[s_]
                        kvh = h // 4
                        bs = 64 * (s_ % 2)
                        c = s_ // 2
                        avb = [bank[2 + 2 * (s_ % 2)], bank[3 + 2 * (s_ % 2)]]
                        sumrow_p = 96 - 64 * (s_ % 2)
                        for kb in range(NB):
                            qlo, nqk = _kb_geom(kb)
                            nst = s_ * NB + kb
                            if w.on("tensor"):
                                if s_ == 0 and kb == 0:
                                    for mb in list(range(8, 24)) + list(range(4)):
                                        waitm(w, ("rope", mb))
                                    for mb in range(4, 8):
                                        waitm(w, ("post", mb))
                                if nst >= 2:
                                    waitm(w, ("exp", nst - 2))
                                w.h.matmul(
                                    bank[nst % 2][:, 0:nqk],
                                    kp[bs:bs + 64,
                                       (kvh // 2) * NTOK + kb * 128:
                                       (kvh // 2) * NTOK + (kb + 1) * 128],
                                    qp[bs:bs + 64, c * NQ + qlo: c * NQ + qlo + nqk],
                                    start=True, stop=True).then_inc(sems["t"], 1)
                            mark(w, ("stm", s_, kb), "t")
                            if w.on("scalar"):
                                waitm(w, ("stm", s_, kb))
                                if s_ == 0 and kb == 0:
                                    waitm(w, "esb")
                                if nst >= 2:
                                    waitm(w, ("avk", nst - 2))
                                w.h.activation(
                                    etb[:, (kb % 2) * 256:(kb % 2) * 256 + nqk],
                                    bank[nst % 2][:, 0:nqk], AF.Exp, bias=negc)
                                w.done("s", 1)
                            mark(w, ("exps", s_, kb), "s")
                            if w.on("vector"):
                                waitm(w, ("exps", s_, kb))
                                msk = m0 if kb == 0 else m1[:, 0:nqk]
                                w.h.tensor_mul(
                                    etb[:, (kb % 2) * 256:(kb % 2) * 256 + nqk],
                                    etb[:, (kb % 2) * 256:(kb % 2) * 256 + nqk],
                                    msk)
                                w.done("v", 1)
                            mark(w, ("exp", nst), "v")
                            if w.on("tensor"):
                                waitm(w, ("exp", nst))
                                if s_ >= 2 and kb == 0:
                                    waitm(w, ("apn", s_ - 2, 1))
                                vloc = vsb[:, (kb * NKV + kvh) * 65:
                                           (kb * NKV + kvh) * 65 + 65]
                                parts = []
                                if kb >= 1:
                                    parts.append((kb - 1, False))
                                if kb <= NB - 2:
                                    parts.append((kb, True))
                                for qb, fresh in parts:
                                    plo = qb * 128
                                    hf = qb // 4
                                    gfirst = fresh and qb == hf * 4
                                    glast = (not fresh) and qb == hf * 4 + 3
                                    esl = etb[:, (kb % 2) * 256 + (plo - qlo):
                                              (kb % 2) * 256 + (plo - qlo) + 128]
                                    w.h.matmul(
                                        avb[hf][bs:bs + 64, plo - hf * 512:
                                                plo - hf * 512 + 128],
                                        vloc[:, 0:64], esl,
                                        start=gfirst, stop=glast,
                                        tile_position=(0, bs))
                                    w.h.matmul(
                                        avb[hf][sumrow_p:sumrow_p + 1,
                                                plo - hf * 512:
                                                plo - hf * 512 + 128],
                                        vloc[:, 64:65], esl,
                                        start=gfirst, stop=glast,
                                        skip_group_check=True,
                                        tile_position=(0, sumrow_p))
                                w.done("t", 1)
                            mark(w, ("avk", nst), "t")
                            if kb == NB - 1:
                                MS[("av", s_)] = MS[("avk", nst)]
                        for hf in range(2):
                            if w.on("scalar"):
                                waitm(w, ("av", s_))
                                w.h.activation(
                                    nrm[sumrow_p:sumrow_p + 1, 0:512],
                                    avb[hf][sumrow_p:sumrow_p + 1, 0:512],
                                    AF.Ln, scale=float(np.exp(LNK)),
                                    bias=es_b[sumrow_p:sumrow_p + 1, s_:s_ + 1])
                                w.dr()
                                w.h.activation(
                                    nrm[sumrow_p:sumrow_p + 1, 0:512],
                                    nrm[sumrow_p:sumrow_p + 1, 0:512],
                                    AF.Exp, scale=-1.0,
                                    bias=tcol[sumrow_p:sumrow_p + 1, 4:5])
                                w.done("s", 1)
                            mark(w, ("invd", s_, hf), "s")
                            if substage == 28 and s_ == SLIM - 1 and hf == 0:
                                if w.on("scalar"):
                                    w.h.copy(scbc[bs:bs + 64, 0:512],
                                             avb[0][bs:bs + 64, 0:512])
                                    w.h.copy(scbc[sumrow_p:sumrow_p + 1, 0:512],
                                             avb[0][sumrow_p:sumrow_p + 1, 0:512])
                                    w.h.copy(scbc[sumrow_p:sumrow_p + 1, 512:1024],
                                             nrm[sumrow_p:sumrow_p + 1, 0:512])
                                    w.done("s", 1)
                                mark(w, "dmp_ev", "s")
                            if w.on("tensor"):
                                waitm(w, ("invd", s_, hf))
                                if (s_, hf) != (0, 0):
                                    pkey = (s_, 0) if hf else (s_ - 1, 1)
                                    waitm(w, ("apn",) + pkey)
                                w.h.matmul(
                                    bank[6][bs:bs + 64, 0:512],
                                    onesf[sumrow_p:sumrow_p + 1, 0:64],
                                    nrm[sumrow_p:sumrow_p + 1, 0:512],
                                    start=True, stop=True,
                                    tile_position=(sumrow_p, bs))\
                                    .then_inc(sems["t"], 1)
                            mark(w, ("nbmm", s_, hf), "t")
                            if w.on("scalar"):
                                waitm(w, ("nbmm", s_, hf))
                                w.h.copy(nrm[bs:bs + 64, 0:512],
                                         bank[6][bs:bs + 64, 0:512])
                                w.done("s", 1)
                            mark(w, ("nbcp", s_, hf), "s")
                            if w.on("vector"):
                                waitm(w, ("nbcp", s_, hf))
                                w.h.tensor_mul(
                                    ap_[bs:bs + 64, c * NQ + hf * 512:
                                        c * NQ + hf * 512 + 512],
                                    avb[hf][bs:bs + 64, 0:512],
                                    nrm[bs:bs + 64, 0:512])
                                w.done("v", 1)
                            mark(w, ("apn", s_, hf), "v")
                        qw_emit(w, 7 * (s_ + 1))
                        if substage == 28 and s_ == SLIM - 1:
                            if w.on("sync"):
                                waitm(w, "dmp_ev")
                                w.h.dma_start(yh[0],
                                              scbc[:, 0:1024]).then_inc(sems["st"], 16)
                            w.inc("st", 16)
                            return

                    if "d_ap" in dbg:
                        if w.on("sync"):
                            for s_ in range(32):
                                waitm(w, ("apn", s_, 1))
                            w.h.dma_start(dbg["d_ap"], ap_[:, :]).then_inc(sems["st"], 16)
                        w.inc("st", 16)

                    for mb in range(16):
                        if w.on("gpsimd"):
                            if mb >= 2:
                                waitm(w, ("owfree", mb - 2))
                            w.h.dma_start(
                                owbuf[:, (mb % 2) * 2048:(mb % 2 + 1) * 2048]
                                .rearrange("p (k n) -> p k n", k=16),
                                outwh[mb].rearrange("(k p) n -> p k n", p=128))\
                                .then_inc(sems["w"], 16)
                        mark(w, ("owld", mb), "w", 16)

                        if w.on("sync"):
                            if mb >= 2:
                                waitm(w, ("oev", 2 * (mb - 2) + 1))
                            w.h.dma_start(
                                xf2[:, (mb % 2) * NQ:(mb % 2 + 1) * NQ],
                                x_fmh[mb][:, HALO:]).then_inc(sems["x"], 16)
                        mark(w, ("xld2", mb), "x", 16)

                        if w.on("vector"):
                            waitm(w, ("xld2", mb))
                            if mb == 0:
                                waitm(w, ("scbc_cp", 3))
                            if mb >= 4:
                                waitm(w, ("qast", mb - 4))
                            xs_ = xf2[:, (mb % 2) * NQ:(mb % 2) * NQ + NQ]
                            w.h.tensor_mul(xnf2[:, 0:NQ], xs_, abc[:, HALO:NTOK])
                            w.h.tensor_add(xnf2[:, NQ:2 * NQ], xnf2[:, 0:NQ],
                                           abc[:, NTOK + HALO:2 * NTOK])
                            w.h.tensor_mul(xnf2[:, 0:NQ], xnf2[:, NQ:2 * NQ],
                                           scbc[:, 0:NQ])
                            w.h.tensor_scalar(qi32[:, :], xnf2[:, 0:NQ],
                                              MAGIC, MAGIC, OP.add, OP.subtract)
                            w.h.tensor_mul(qtr[:, (mb % 4) * NQ:(mb % 4 + 1) * NQ],
                                           qi32[:, :], scbc[:, NQ:2 * NQ])
                            w.done("v", 1)
                        mark(w, ("qich", mb), "v")
                        if w.on("sync"):
                            waitm(w, ("qich", mb))
                            w.h.dma_start(qa_hh[mb],
                                          qtr[:, (mb % 4) * NQ:(mb % 4 + 1) * NQ])\
                                .then_inc(sems["x"], 16)
                        mark(w, ("qast", mb), "x", 16)

                        for si, lo in enumerate((0, 512)):
                            nseg = 2 * mb + si
                            if w.on("tensor"):
                                waitm(w, ("owld", mb))
                                if nseg == 0:
                                    for s_ in range(32):
                                        waitm(w, ("apn", s_, 1))
                                if nseg >= 2:
                                    waitm(w, ("oev", nseg - 2))
                                bk = bank[nseg % 2]
                                for kc in range(16):
                                    w.h.matmul(
                                        bk[:, 0:512],
                                        owbuf[:, (mb % 2) * 2048 + kc * 128:
                                              (mb % 2) * 2048 + (kc + 1) * 128],
                                        ap_[:, kc * NQ + lo: kc * NQ + lo + 512],
                                        start=(kc == 0), stop=False)
                                w.h.matmul(
                                    bk[:, 0:512], obias[:, mb * 128:(mb + 1) * 128],
                                    onesr[:, 0:512], start=False, stop=True)\
                                    .then_inc(sems["t"], 1)
                            mark(w, ("omm", nseg), "t")
                            if w.on("vector"):
                                waitm(w, ("omm", nseg))
                                if si == 0 and mb >= 2:
                                    waitm(w, ("accst", mb - 2))
                                w.h.tensor_add(
                                    accb[:, (mb % 2) * NQ + lo:(mb % 2) * NQ + lo + 512],
                                    bank[nseg % 2][:, 0:512],
                                    xf2[:, (mb % 2) * NQ + lo:
                                        (mb % 2) * NQ + lo + 512])
                                w.done("v", 1)
                            mark(w, ("oev", nseg), "v")
                        MS[("owfree", mb)] = MS[("omm", 2 * mb + 1)]
                        if w.on("sync"):
                            waitm(w, ("oev", 2 * mb + 1))
                            w.h.dma_start(acc_hh[mb], accb[:, (mb % 2) * NQ:
                                                           (mb % 2 + 1) * NQ])\
                                .then_inc(sems["x"], 16)
                        mark(w, ("accst", mb), "x", 16)
                    qw_emit(w, NQU)
                    if "d_acc" in dbg:
                        if w.on("sync"):
                            for mb in range(16):
                                waitm(w, ("accst", mb))
                            w.h.dma_start(dbg["d_acc"], acc_hh).then_inc(sems["st"], 16)
                        w.inc("st", 16)
                    if "d_qi" in dbg:
                        if w.on("sync"):
                            for mb in range(16):
                                waitm(w, ("qast", mb))
                            w.h.dma_start(
                                dbg["d_qi"].rearrange("p (k n) -> p k n", k=16),
                                qa_hh.rearrange("k p n -> p k n"))\
                                .then_inc(sems["st"], 16)
                        w.inc("st", 16)
                    if "d_wqg" in dbg:
                        if w.on("sync"):
                            waitm(w, ("qwqw", NGU_G - 1))
                            w.h.dma_start(dbg["d_wqg"], wq_gh).then_inc(sems["st"], 16)
                        w.inc("st", 16)
                phase(ph_B2)
        sB.close()

        if on_st("full"):
            def _subs(G):
                out, i = [], 0
                for sz in (3, 3, 2):
                    if i < len(G):
                        out.append(list(range(i, min(i + sz, len(G)))))
                        i += sz
                return out

            with contextlib.ExitStack() as sC1:
                def sbC(name, shape, dtype=F32):
                    return sC1.enter_context(nc.sbuf_tensor(name, list(shape), dtype))

                qib = sbC("qib", [128, 16 * NQ], BF16)
                slab = sbC("slab", [128, 16 * 8 * 128], BF16)
                sg = sbC("sg", [128, 8 * NQ], F32)
                hch = sbC("hch", [128, 2 * NQ], F32)
                amaxr = sbC("amaxr", [128, NQ], F32)
                hscb = sbC("hscb", [128, 2 * NQ], F32)
                habs = sbC("habs", [128, NQ], F32)
                hi32 = sbC("hi32", [128, NQ], I32)
                qhbuf = sbC("qhbuf", [128, 2 * NQ], BF16)

                def ph_C1(w):
                    dma_fence(w)
                    if w.on("sync"):
                        for mb in range(16):
                            waitm(w, ("qast", mb))
                        for mb in range(16):
                            w.h.dma_start(qib[:, mb * NQ:(mb + 1) * NQ],
                                          qa_hh[mb]).then_inc(sems["hr"], 16)
                    mark(w, "qald", "hr", 16 * 16)
                    if w.on("vector"):
                        w.h.memset(amaxr[:, :], 0.0)
                        w.done("v", 1)
                    mark(w, "amax0", "v")
                    nsl = 0
                    nh_count = 0
                    for gi, G in enumerate(MGRP):
                        ng = len(G)
                        subs = _subs(G)
                        for proj in range(2):
                            srcw = wq_gh if proj == 0 else wq_uh
                            if w.on("sync"):
                                waitm(w, ("qwqw", NGU_G - 1 if proj == 0
                                          else 2 * NGU_G - 1))
                                if nsl >= 1:
                                    waitm(w, ("slfree", nsl - 1))
                                for kc in range(16):
                                    w.h.dma_start_transpose(
                                        slab[:, kc * ng * 128:(kc + 1) * ng * 128],
                                        srcw[G[0] * 128:(G[-1] + 1) * 128,
                                             kc * 128:(kc + 1) * 128])\
                                        .then_inc(sems["sl"], 16)
                            mark(w, ("slld", nsl), "sl", 16 * 16)
                            for bi, sub in enumerate(subs):
                                if w.on("tensor"):
                                    waitm(w, ("slld", nsl))
                                    if (gi, proj, bi) == (0, 0, 0):
                                        waitm(w, "qald")
                                    if proj == 0:
                                        if bi > 0:
                                            waitm(w, ("silu", nsl, bi - 1))
                                        elif gi > 0:
                                            waitm(w, ("ghev", gi - 1))
                                    else:
                                        waitm(w, ("silu", nsl - 1, bi))
                                        if bi > 0:
                                            waitm(w, ("hevs", nsl, bi - 1))
                                    for kc in range(16):
                                        for mi, mloc in enumerate(sub):
                                            for si2 in range(2):
                                                w.h.matmul(
                                                    bank[2 * mi + si2][:, 0:512],
                                                    slab[:, kc * ng * 128 + mloc * 128:
                                                         kc * ng * 128 + (mloc + 1) * 128],
                                                    qib[:, kc * NQ + si2 * 512:
                                                        kc * NQ + si2 * 512 + 512],
                                                    start=(kc == 0), stop=(kc == 15))
                                    w.done("t", 1)
                                mark(w, ("fmm", nsl, bi), "t")
                                if proj == 0:
                                    if w.on("scalar"):
                                        waitm(w, ("fmm", nsl, bi))
                                        for mi, mloc in enumerate(sub):
                                            for si2 in range(2):
                                                w.h.activation(
                                                    sg[:, mloc * NQ + si2 * 512:
                                                       mloc * NQ + si2 * 512 + 512],
                                                    bank[2 * mi + si2][:, 0:512],
                                                    AF.Silu)
                                        w.done("s", 1)
                                    mark(w, ("silu", nsl, bi), "s")
                                else:
                                    for mi, mloc in enumerate(sub):
                                        nh_ = nh_count
                                        if w.on("vector"):
                                            waitm(w, ("fmm", nsl, bi))
                                            waitm(w, ("silu", nsl - 1, bi))
                                            if nh_ == 0:
                                                waitm(w, "amax0")
                                            if nh_ >= 2:
                                                waitm(w, ("hsp", nh_ - 2))
                                            hsl = hch[:, (nh_ % 2) * NQ:
                                                      (nh_ % 2 + 1) * NQ]
                                            for si2 in range(2):
                                                w.h.tensor_mul(
                                                    hsl[:, si2 * 512:si2 * 512 + 512],
                                                    sg[:, mloc * NQ + si2 * 512:
                                                       mloc * NQ + si2 * 512 + 512],
                                                    bank[2 * mi + si2][:, 0:512])
                                            w.done("v", 1)
                                        mark(w, ("hev", nh_), "v")
                                        if w.on("scalar"):
                                            waitm(w, ("hev", nh_))
                                            if nh_ >= 1:
                                                waitm(w, ("hmax", nh_ - 1))
                                            w.h.activation(habs[:, :], hch[:, (nh_ % 2) * NQ:
                                                           (nh_ % 2 + 1) * NQ], AF.Abs)
                                            w.done("s", 1)
                                        mark(w, ("habs", nh_), "s")
                                        if w.on("vector"):
                                            waitm(w, ("habs", nh_))
                                            w.h.tensor_tensor(
                                                amaxr[:, :], amaxr[:, :],
                                                habs[:, :], OP.max)
                                            w.done("v", 1)
                                        mark(w, ("hmax", nh_), "v")
                                        if w.on("sync"):
                                            waitm(w, ("hev", nh_))
                                            w.h.dma_start(
                                                hsp_hh[G[0] + mloc],
                                                hch[:, (nh_ % 2) * NQ:
                                                    (nh_ % 2 + 1) * NQ])\
                                                .then_inc(sems["sp"], 16)
                                        mark(w, ("hsp", nh_), "sp", 16)
                                        nh_count += 1
                                    MS[("hevs", nsl, bi)] = MS[("hev", nh_count - 1)]
                            MS[("slfree", nsl)] = MS[("fmm", nsl, len(subs) - 1)]
                            nsl += 1
                        MS[("ghev", gi)] = MS[("hev", nh_count - 1)]
                    MS["hev_all"] = MS[("hmax", GATE_MC - 1)]
                phase(ph_C1)

                def ph_C2(w):
                    for tb in range(8):
                        if w.on("tensor"):
                            waitm(w, "hev_all")
                            if tb > 0:
                                waitm(w, ("hsc_rm", tb - 1))
                            w.h.transpose(psR[:, 0:128],
                                          amaxr[:, tb * 128:(tb + 1) * 128], identf)
                            w.done("t", 1)
                        mark(w, ("hsc_tr", tb), "t")
                        if w.on("vector"):
                            waitm(w, ("hsc_tr", tb))
                            w.h.tensor_reduce(tcol[:, 0:1], psR[:, 0:128], AX.X, OP.max)
                            w.dr()
                            w.h.tensor_scalar(scq[:, tb:tb + 1], tcol[:, 0:1],
                                              EPS, None, OP.max)
                            w.dr()
                            w.h.reciprocal(tcol[:, 2:3], scq[:, tb:tb + 1])
                            w.dr()
                            w.h.tensor_scalar(scq[:, 8 + tb:9 + tb], tcol[:, 2:3],
                                              127.0, None, OP.mult)
                            w.dr()
                            w.h.reciprocal(scq[:, 16 + tb:17 + tb],
                                           scq[:, 8 + tb:9 + tb])
                            w.dr()
                            w.done("v", 1)
                        mark(w, ("hsc_rm", tb), "v")
                    cols = [(scq[:, 8 + j:9 + j], scrow[:, j * 128:(j + 1) * 128])
                            for j in range(8)]
                    cols += [(scq[:, 16 + j:17 + j],
                              scrow[:, NQ + j * 128: NQ + (j + 1) * 128])
                             for j in range(8)]
                    for g0 in range(0, len(cols), 3):
                        grp = cols[g0:g0 + 3]
                        if w.on("tensor"):
                            if g0 == 0:
                                waitm(w, ("hsc_rm", 7))
                            else:
                                waitm(w, ("hrowcp", g0 - 3))
                            for ci, (src, _d) in enumerate(grp):
                                w.h.transpose(psR[0:1, ci * 128:(ci + 1) * 128],
                                              src, identf)
                            w.done("t", 1)
                        mark(w, ("hrowtr", g0), "t")
                        if w.on("scalar"):
                            waitm(w, ("hrowtr", g0))
                            for ci, (_s, dst) in enumerate(grp):
                                dst_ = dst
                                w.h.copy(dst_, psR[0:1, ci * 128:(ci + 1) * 128])
                            w.done("s", 1)
                        mark(w, ("hrowcp", g0), "s")
                    for t_ in range(4):
                        half = t_ // 2
                        off = (t_ % 2) * 512
                        if w.on("tensor"):
                            if t_ == 0:
                                waitm(w, ("hrowcp", 15))
                            else:
                                waitm(w, ("hscbc_cp", t_ - 1))
                            w.h.matmul(
                                bank[5][:, 0:512], onesf[0:1, :],
                                scrow[:, half * NQ + off: half * NQ + off + 512],
                                start=True, stop=True).then_inc(sems["t"], 1)
                        mark(w, ("hscbc_mm", t_), "t")
                        if w.on("vector"):
                            waitm(w, ("hscbc_mm", t_))
                            w.h.tensor_copy(
                                hscb[:, half * NQ + off: half * NQ + off + 512],
                                bank[5][:, 0:512])
                            w.done("v", 1)
                        mark(w, ("hscbc_cp", t_), "v")
                    MS["hscb"] = MS[("hscbc_cp", 3)]

                    for kc in range(GATE_MC):
                        if w.on("sync"):
                            waitm(w, ("hsp", kc))
                            if kc >= 2:
                                waitm(w, ("qhq", kc - 2))
                            w.h.dma_start(hch[:, (kc % 2) * NQ:(kc % 2 + 1) * NQ],
                                          hsp_hh[kc]).then_inc(sems["hr"], 16)
                        mark(w, ("hrd", kc), "hr", 16)
                        if w.on("vector"):
                            waitm(w, ("hrd", kc))
                            if kc == 0:
                                waitm(w, "hscb")
                            if kc >= 2:
                                waitm(w, ("qhw", kc - 2))
                            hsl = hch[:, (kc % 2) * NQ:(kc % 2 + 1) * NQ]
                            w.h.tensor_mul(sg[:, 0:NQ], hsl, hscb[:, 0:NQ])
                            w.h.tensor_scalar(hi32[:, :], sg[:, 0:NQ],
                                              MAGIC, MAGIC, OP.add, OP.subtract)
                            w.h.tensor_mul(qhbuf[:, (kc % 2) * NQ:(kc % 2 + 1) * NQ],
                                           hi32[:, :], hscb[:, NQ:2 * NQ])
                            w.done("v", 1)
                        mark(w, ("qhq", kc), "v")
                        if w.on("sync"):
                            waitm(w, ("qhq", kc))
                            for tt in range(4):
                                w.h.dma_start(
                                    qh_hh[tt, kc],
                                    qhbuf[:, (kc % 2) * NQ + tt * 256:
                                          (kc % 2) * NQ + (tt + 1) * 256])\
                                    .then_inc(sems["hr"], 16)
                        mark(w, ("qhw", kc), "hr", 16 * 4)
                phase(ph_C2)

            with contextlib.ExitStack() as sC3:
                def sbC3(name, shape, dtype=F32):
                    return sC3.enter_context(nc.sbuf_tensor(name, list(shape), dtype))

                dslabs = sbC3("dslabs", [128, GATE_MC * 1024], BF16)
                qhT = sbC3("qhT", [128, 2 * GATE_MC * 256], BF16)
                accb2 = sbC3("accb2", [128, 8 * 256], F32)
                outsb = sbC3("outsb", [128, 8 * 256], F32)

                def ph_C3(w):
                    dma_fence(w)
                    for mh in range(2):
                        if w.on("sync"):
                            waitm(w, ("qwqw", NQU - 1))
                            if mh > 0:
                                waitm(w, ("dout", mh - 1, 3))
                            for kc in range(GATE_MC):
                                w.h.dma_start_transpose(
                                    dslabs[:, kc * 1024:(kc + 1) * 1024],
                                    wq_dh[mh * 1024:(mh + 1) * 1024,
                                          kc * 128:(kc + 1) * 128])\
                                    .then_inc(sems["sl"], 16)
                        mark(w, ("dsl", mh), "sl", 16 * GATE_MC)
                        for tt in range(4):
                            nt = 4 * mh + tt
                            if w.on("sync"):
                                waitm(w, ("qhw", GATE_MC - 1))
                                if nt >= 2:
                                    waitm(w, ("dmm", nt - 2))
                                w.h.dma_start(
                                    qhT[:, (nt % 2) * GATE_MC * 256:
                                        (nt % 2 + 1) * GATE_MC * 256]
                                    .rearrange("p (k n) -> p k n", k=GATE_MC),
                                    qh_hh[tt].rearrange("k p n -> p k n"))\
                                    .then_inc(sems["hr"], 16)
                            mark(w, ("qhT", nt), "hr", 16)
                            if w.on("sync"):
                                if nt >= 1:
                                    waitm(w, ("dout", *divmod(nt - 1, 4)))
                                for mi in range(8):
                                    w.h.dma_start(
                                        accb2[:, mi * 256:(mi + 1) * 256],
                                        acc_hh[mh * 8 + mi][:, tt * 256:(tt + 1) * 256])\
                                        .then_inc(sems["hr"], 16)
                            mark(w, ("acc2", nt), "hr", 16 * 8)
                            if w.on("tensor"):
                                waitm(w, ("dsl", mh))
                                waitm(w, ("qhT", nt))
                                if nt >= 1:
                                    waitm(w, ("dout", *divmod(nt - 1, 4)))
                                for kc in range(GATE_MC):
                                    for mi in range(8):
                                        bk8 = bank[mi] if mi < 7 else psV
                                        w.h.matmul(
                                            bk8[:, 0:256],
                                            dslabs[:, kc * 1024 + mi * 128:
                                                   kc * 1024 + (mi + 1) * 128],
                                            qhT[:, (nt % 2) * GATE_MC * 256 + kc * 256:
                                                (nt % 2) * GATE_MC * 256 + (kc + 1) * 256],
                                            start=(kc == 0), stop=(kc == GATE_MC - 1))
                                w.done("t", 1)
                            mark(w, ("dmm", nt), "t")
                            if w.on("vector"):
                                waitm(w, ("dmm", nt))
                                waitm(w, ("acc2", nt))
                                for mi in range(8):
                                    bk8 = bank[mi] if mi < 7 else psV
                                    w.h.tensor_add(
                                        outsb[:, mi * 256:(mi + 1) * 256],
                                        bk8[:, 0:256],
                                        accb2[:, mi * 256:(mi + 1) * 256])
                                w.done("v", 1)
                            mark(w, ("dout", mh, tt), "v")
                            if w.on("sync"):
                                waitm(w, ("dout", mh, tt))
                                for mi in range(8):
                                    w.h.dma_start(
                                        yh[mh * 8 + mi][:, tt * 256:(tt + 1) * 256],
                                        outsb[:, mi * 256:(mi + 1) * 256])\
                                        .then_inc(sems["st"], 16)
                            mark(w, ("yst", mh, tt), "st", 16 * 8)
                    if w.on("sync"):
                        for sn in ("c", "x", "w", "fw", "wqw", "sl", "sp",
                                   "hr", "st"):
                            if w.c[sn]:
                                w.h.wait_ge(sems[sn], w.c[sn])
                phase(ph_C3)
        else:
            if substage in (141, 28):
                def ph_W(w):
                    dma_fence(w, ENGS)
                phase(ph_W)
            else:
             with contextlib.ExitStack() as sP:
                 zb = sP.enter_context(nc.sbuf_tensor("zb", [128, NQ], F32))

                 def ph_Z(w):
                     dma_fence(w, ENGS)
                     if w.on("vector"):
                         w.h.memset(zb[:, :], 0.0)
                         w.done("v", 1)
                     mark(w, "zro", "v")
                     if w.on("sync"):
                         waitm(w, "zro")
                         for mb in range(16):
                             w.h.dma_start(yh[mb], zb[:, :]).then_inc(sems["st"], 16)
                     w.inc("st", 16 * 16)
                     if w.on("sync"):
                         for sn in ("c", "x", "w", "fw", "wqw", "sl", "sp",
                                    "hr", "st"):
                             if w.c[sn]:
                                 w.h.wait_ge(sems[sn], w.c[sn])
                 phase(ph_Z)
    return nc



def _host_inputs(inputs):
    x = np.asarray(inputs["L0_attn_x"], np.float32)
    qkv_w = np.asarray(inputs["L0_attn_qkv_w"], np.float32)
    qkv_b = np.asarray(inputs["L0_attn_qkv_b"], np.float32)
    out_w = np.asarray(inputs["L0_attn_out_w"], np.float32)
    out_b = np.asarray(inputs["L0_attn_out_b"], np.float32)
    sinks = np.asarray(inputs["L0_attn_sinks"], np.float32)
    w_gate = np.asarray(inputs["L0_ffn_w_gate"], np.float32)
    w_up = np.asarray(inputs["L0_ffn_w_up"], np.float32)
    w_down = np.asarray(inputs["L0_ffn_w_down"], np.float32)

    qperm = np.concatenate([np.arange(h * 64, h * 64 + 64) for h in PERM_H])
    wq_rows = qkv_w[qperm]
    wk_rows = qkv_w[2048:2560]
    wv_rows = qkv_w[2560:3072]
    worder = np.concatenate([wk_rows, wv_rows, wq_rows], axis=0)
    qkvw_blocks = np.ascontiguousarray(
        worder.T.reshape(D, 24, 128).transpose(1, 0, 2))
    b_order = np.concatenate([qkv_b[2048:2560], qkv_b[2560:3072], qkv_b[qperm]])
    qkvb_row = np.ascontiguousarray(b_order.reshape(1, 24 * 128)).astype(BF)
    outw_blocks = np.ascontiguousarray(
        out_w[:, qperm].T.reshape(D, 16, 128).transpose(1, 0, 2))
    outb_row = np.ascontiguousarray(out_b.reshape(1, 16 * 128)).astype(BF)
    sinks_slot = np.ascontiguousarray(sinks[PERM_H].reshape(1, 32))

    wvt_blocks = np.ascontiguousarray(
        wv_rows.T.reshape(16, 128, 512))
    wg_blocks = np.ascontiguousarray(w_gate.reshape(GATE_MC, 128, D))
    wu_blocks = np.ascontiguousarray(w_up.reshape(GATE_MC, 128, D))
    wd_blocks = np.ascontiguousarray(w_down.reshape(16, 128, DFF))

    cos_t, sin_t = _rope_tables()
    dm = np.arange(128) % 64
    sgn = np.where(dm < 32, -1.0, 1.0).astype(np.float32)
    didx = (np.arange(128) % 32)

    def mk_tables(pos, scale):
        ct = cos_t[pos][:, didx].T * scale
        st_ = sin_t[pos][:, didx].T * (sgn[:, None] * scale)
        return ct.astype(np.float32), st_.astype(np.float32)

    ident = np.eye(128)
    psw = np.zeros((128, 128), np.float32)
    for blk in range(2):
        for d in range(32):
            psw[blk * 64 + d, blk * 64 + 32 + d] = 1.0
            psw[blk * 64 + 32 + d, blk * 64 + d] = 1.0
    kk = np.arange(128)[:, None]
    qq = np.arange(256)[None, :]
    m1 = ((qq >= kk) & (qq <= kk + 127)).astype(np.float32)
    m0_half1 = (qq[:, :128] < kk).astype(np.float32)

    consts = dict(
        qkvw=qkvw_blocks, qkvb=qkvb_row, outw=outw_blocks, outb=outb_row,
        wg=wg_blocks, wu=wu_blocks, wd=wd_blocks,
        pswap=psw.astype(BF), pswapf=psw.astype(np.float32),
        identb=ident.astype(BF), identf=ident.astype(np.float32),
        onesf=np.ones((128, 128), np.float32),
        onesr=np.ones((1, 512), BF), onescol=np.ones((128, 1), np.float32),
        m1=m1.astype(BF), sinks=sinks_slot, wvt=wvt_blocks,
    )

    in_maps = []
    for core in range(NCORES):
        b, half = divmod(core, 2)
        t0 = half * NQ
        xs = x[b]
        x_tm = np.zeros((NTOK, D), np.float32)
        if half == 1:
            x_tm[:] = xs[t0 - HALO: t0 + NQ]
        else:
            x_tm[HALO:] = xs[0:NQ]
        x_fm = np.ascontiguousarray(x_tm.T)
        qpos = np.arange(t0, t0 + NQ)
        kpos = np.clip(np.arange(t0 - HALO, t0 + NQ), 0, T - 1)
        cq_c, sq_c = mk_tables(qpos, SM)
        ckc, skc = mk_tables(kpos, 1.0)
        m0c = m0_half1 if half == 1 else np.zeros((128, 128), np.float32)
        im = dict(consts)
        im["x_fm"] = np.ascontiguousarray(x_fm.reshape(16, 128, NTOK))
        im["x_tm"] = np.ascontiguousarray(x_tm.reshape(NB, 128, D))
        im["cq"] = np.ascontiguousarray(np.concatenate([cq_c, sq_c], axis=1))
        im["ck"] = np.ascontiguousarray(np.concatenate([ckc, skc], axis=1))
        im["m0"] = m0c.astype(BF)
        in_maps.append(im)
    return in_maps


_NC_CACHE = {}


def kernel(**inputs):
    key = (STAGE, tuple(sorted(DEBUG)))
    if key not in _NC_CACHE:
        _NC_CACHE[key] = build_nc(STAGE, DEBUG)
    nc = _NC_CACHE[key]
    in_maps = _host_inputs(inputs)
    res = run_bass_kernel_spmd(nc, in_maps, core_ids=list(range(NCORES)))
    y = np.empty((B, T, D), np.float32)
    for core in range(NCORES):
        b, half = divmod(core, 2)
        t0 = half * NQ
        yc = np.asarray(res.results[core]["y"])
        y[b, t0:t0 + NQ, :] = yc.reshape(D, NQ).T
    if DEBUG:
        kernel.debug_results = res.results
    return y
